# version 20
# baseline (speedup 1.0000x reference)
import numpy as np
import ml_dtypes
import concourse.bass as bass
import concourse.mybir as mybir
from concourse import bass_isa
from concourse.bass_utils import run_bass_kernel_spmd

F32 = mybir.dt.float32
BF16 = mybir.dt.bfloat16
AF = mybir.ActivationFunctionType
OP = mybir.AluOpType

MAGIC = float(np.float32(3 * 2**22))
PI = float(np.float32(np.pi))
PI2 = float(np.float32(np.pi / 2))
PI4 = float(np.float32(np.pi / 4))

D = 1024
T = 2048
TC = 512
NCH = T // TC
ND = D // 128
DI = 2048
NDI = DI // 128
DS = 16
FF = 4096
NFF = FF // 128
H = 16
DH = 64
INV_D = float(np.float32(1.0) / np.float32(D))
INV127 = float(np.float32(1.0) / np.float32(127.0))

ENGS = ("tensor", "vector", "scalar", "gpsimd", "sync")
DBG_ON = False
DRAIN_ALL = False


class KB:
    def __init__(self, nc):
        self.nc = nc
        self.cms = []
        self.sem_handles = {}
        self.counts = {}
        self.queue = {e: [] for e in ENGS}
        self.res = {}
        self.waited = {e: {} for e in ENGS}
        for e in ENGS:
            self._mksem("c_" + e)
        for e in ("sync", "gpsimd", "scalar"):
            self._mksem("d_" + e)
        for st_ in ("wl0", "wl1", "wl20", "wl21", "rep", "row"):
            self._mksem("d_" + st_)
        self.referenced = {}

    def _mksem(self, name):
        cm = self.nc.semaphore(name)
        self.sem_handles[name] = cm.__enter__()
        self.cms.append((name, cm))
        self.counts[name] = 0

    def sbuf(self, name, shape, dtype):
        cm = self.nc.sbuf_tensor(name, list(shape), dtype)
        h = cm.__enter__()
        self.cms.append((name, cm))
        return h

    def psum(self, name, shape, dtype=F32):
        cm = self.nc.psum_tensor(name, list(shape), dtype)
        h = cm.__enter__()
        self.cms.append((name, cm))
        return h

    def free(self, *names):
        idx = []
        for nm in names:
            for i, (n, cm) in enumerate(self.cms):
                if n == nm:
                    idx.append(i)
                    break
            else:
                raise KeyError(nm)
        for i in sorted(idx, reverse=True):
            n, cm = self.cms[i]
            cm.__exit__(None, None, None)
            del self.cms[i]

    def close(self):
        for n, cm in reversed(self.cms):
            cm.__exit__(None, None, None)
        self.cms = []

    DRAIN_K = 3

    def op(self, eng, fn, reads=(), writes=(), dma=False, inc=1, stream=None,
           drained=False):
        sem = (("d_" + stream) if stream else ("d_" + eng)) if dma else ("c_" + eng)
        inc_amt = 16 * inc if dma else inc
        own = "c_" + eng
        waits = {}
        need_drain = [False]

        def need(s, c, prod_drained=False):
            if s == own:
                if (not dma and eng in ("vector", "scalar", "gpsimd")
                        and not prod_drained
                        and self.counts[own] - c < self.DRAIN_K):
                    need_drain[0] = True
                return
            if s.startswith("d_"):
                c = max(c, self.counts[s])
            if waits.get(s, 0) < c:
                waits[s] = c

        for k in reads:
            st = self.res.get(k)
            if st and st["w"]:
                need(st["w"][0], st["w"][1], st.get("dr", False))
        for k in writes:
            st = self.res.get(k)
            if st:
                if st["w"]:
                    need(st["w"][0], st["w"][1], st.get("dr", False))
                for s, c in st["r"].items():
                    need(s, c)
        for s, c in waits.items():
            if s.startswith("d_"):
                self.referenced.setdefault(s, set()).add(c)
        prior = self.counts[sem] if dma else 0
        self.counts[sem] += inc_amt
        stamp = (sem, self.counts[sem])
        for k in writes:
            self.res[k] = {"w": stamp, "r": {}, "dr": drained}
        for k in reads:
            st = self.res.setdefault(k, {"w": None, "r": {}})
            if st["r"].get(sem, 0) < stamp[1]:
                st["r"][sem] = stamp[1]
        if need_drain[0]:
            ofn = fn

            def fn(h, ofn=ofn):
                h.drain()
                return ofn(h)
        if DRAIN_ALL and not dma and eng in ("vector", "scalar", "gpsimd"):
            ofn2 = fn

            def fn(h, ofn2=ofn2):
                ofn2(h)
                return h.drain()
        self.queue[eng].append((waits, fn, sem, inc_amt, prior))
        return stamp

    def wait_only(self, eng, reads):
        waits = {}
        own = "c_" + eng
        for k in reads:
            st = self.res.get(k)
            if st and st["w"]:
                s, c = st["w"]
                if s != own and waits.get(s, 0) < c:
                    waits[s] = c
        for s, c in waits.items():
            if s.startswith("d_"):
                self.referenced.setdefault(s, set()).add(c)
        self.queue[eng].append((waits, None, None, 0, 0))

    def barrier(self):
        tot = {s: c for s, c in self.counts.items() if c > 0}
        for e in ENGS:
            waits = {s: c for s, c in tot.items() if s != "c_" + e}
            for s, c in waits.items():
                if s.startswith("d_"):
                    self.referenced.setdefault(s, set()).add(c)
            self.queue[e].append((waits, None, None, 0, 0))
        for k in self.res:
            self.res[k] = {"w": None, "r": {}}

    def dma(self, eng, out_ap, in_ap, reads=(), writes=(), slow=False, stream=None):
        if slow:
            fn = lambda h: h.dma_start(out_ap, in_ap, allow_slow_non_contiguous=True)
        else:
            fn = lambda h: h.dma_start(out_ap, in_ap)
        return self.op(eng, fn, reads=reads, writes=writes, dma=True, stream=stream)

    def flush(self, block):
        for e in ENGS:
            q = self.queue[e]
            if not q:
                continue
            self.queue[e] = []
            deco = getattr(block, e)

            def section(h, q=q, e=e):
                wcache = self.waited[e]
                for waits, fn, sem, inc, prior in q:
                    if (prior > 0 and sem.startswith("d_")
                            and prior in self.referenced.get(sem, ())
                            and wcache.get(sem, 0) < prior):
                        waits = dict(waits)
                        if waits.get(sem, 0) < prior:
                            waits[sem] = prior
                    for s, c in waits.items():
                        if wcache.get(s, 0) >= c:
                            continue
                        h.wait_ge(self.sem_handles[s], c)
                        wcache[s] = c
                    if fn is not None:
                        fn(h).then_inc(self.sem_handles[sem], inc)
            deco(section)


def lhsT_ap(dram_ap, ld, row0, col0, nk):
    return bass.AP(dram_ap.tensor, row0 * ld + col0,
                   [[ld, 128], [128 * ld, nk], [1, 128]])


def build_program(blocks=("a", "f", "m", "g")):
    nc = bass.Bass("TRN2", target_bir_lowering=False)
    kb = KB(nc)
    dI = {}

    def din(name, shape, dtype=F32):
        dI[name] = nc.dram_tensor(name, list(shape), dtype, kind="ExternalInput").ap()
        return dI[name]

    din("x", (D, T))
    out = nc.dram_tensor("y", [D, T], F32, kind="ExternalOutput").ap()
    din("wqk", (D, 2 * H))
    din("wv", (D, H * DH))
    din("wo", (H * DH, D))
    din("w_fc_n", (FF, D))
    din("w_fc_t", (D, FF))
    din("w_proj_n", (D, FF))
    din("w_proj_t", (FF, D))
    din("u_t", (16, FF))
    din("v_t", (D, 16))
    din("fb_t", (16, 16))
    din("gate", (1, 1))
    din("rms_w", (128, ND))
    din("in_proj", (D, 2 * DI))
    din("convw", (128, NDI * 4))
    din("conv_b", (128, NDI))
    din("x_proj", (DI, 96))
    din("dt_w", (64, DI))
    din("dt_b", (128, NDI))
    din("a_log", (NDI, 128, DS))
    din("dcol", (128, NDI))
    din("w_out", (DI, D))
    din("wg_n", (FF, D))
    din("wg_t", (D, FF))
    din("wu_n", (FF, D))
    din("wu_t", (D, FF))
    din("wd_n", (D, FF))
    din("wd_t", (FF, D))
    din("eye", (128, 128), BF16)
    din("ones", (128, 1), BF16)

    scr = {}

    def dscr(name, shape, dtype=F32):
        kind = "ExternalOutput" if DBG_ON else "Internal"
        scr[name] = nc.dram_tensor("s_" + name, list(shape), dtype, kind=kind).ap()
        return scr[name]

    dscr("rows", (4, T))
    dscr("w_fc_e", (D, FF), BF16)
    dscr("w_proj_e", (FF, D), BF16)
    dscr("wg_e", (D, FF), BF16)
    dscr("wu_e", (D, FF), BF16)
    dscr("wd_e", (FF, D), BF16)
    dscr("sc_fc", (FF,))
    dscr("sc_proj", (D,))
    dscr("qhead", (4, H, T))
    dscr("bc", (2, DS, T), BF16)
    dscr("p1", (128, 32))
    dscr("p2", (4,))

    stream = kb.sbuf("stream", [128, ND * T], F32)
    xnbf = kb.sbuf("xnbf", [128, ND * TC], BF16)
    blkout = kb.sbuf("blkout", [128, ND * TC], F32)
    rowbuf = kb.sbuf("rowbuf", [1, 5 * TC], F32)
    rep = kb.sbuf("rep", [128, 3 * TC], F32)
    xbfs = kb.sbuf("xbfs", [128, TC], BF16)
    x2bf = kb.sbuf("x2bf", [128, TC], BF16)
    vtmp = kb.sbuf("vtmp", [128, TC], F32)
    ones_bf = kb.sbuf("ones_bf", [128, 1], BF16)
    eye_bf = kb.sbuf("eye_bf", [128, 128], BF16)
    eye32 = kb.sbuf("eye32", [128, 128], F32)
    gcol = kb.sbuf("gcol", [128, 1], F32)
    convb_c = kb.sbuf("convb_c", [128, NDI], F32)
    dtb_c = kb.sbuf("dtb_c", [128, NDI], F32)
    rms_c = kb.sbuf("rms_c", [128, ND], F32)
    acol = kb.sbuf("acol", [128, NDI * DS], F32)
    hcar = kb.sbuf("hcar", [128, NDI * DS], F32)
    cumcar = kb.sbuf("cumcar", [128, 18], F32)

    cconst = kb.sbuf("cconst", [128, 4], F32)
    ps = [kb.psum(f"ps{i}", [128, TC]) for i in range(4)]
    ps_row = kb.psum("ps_row", [1, 2 * TC])
    ps_q = kb.psum("ps_q", [128, TC])

    def v(fn, reads=(), writes=(), drained=False):
        return kb.op("vector", fn, reads, writes, drained=drained)

    def s(fn, reads=(), writes=()):
        return kb.op("scalar", fn, reads, writes)

    def t(fn, reads=(), writes=()):
        return kb.op("tensor", fn, reads, writes)

    def g(fn, reads=(), writes=(), drained=False):
        return kb.op("gpsimd", fn, reads, writes, drained=drained)

    kb.dma("sync", ones_bf[:, :], dI["ones"], writes=["ones"])
    kb.dma("sync", eye_bf[:, :], dI["eye"], writes=["eye"])
    v(lambda h: h.tensor_copy(eye32[:, :], eye_bf[:, :]), ["eye"], ["eye32"])
    kb.dma("sync", convb_c[:, :], dI["conv_b"], writes=["convb"])
    kb.dma("sync", dtb_c[:, :], dI["dt_b"], writes=["dtb"])
    kb.dma("sync", rms_c[:, :], dI["rms_w"], writes=["rms"])
    kb.dma("sync", gcol[:, :], dI["gate"].partition_broadcast(128), writes=["gatec"])
    for i in range(ND):
        kb.dma("sync", stream[:, i * T:(i + 1) * T], dI["x"][i * 128:(i + 1) * 128, :],
               writes=[("st", i)])
    for dt in range(NDI):
        kb.dma("sync", acol[:, dt * DS:(dt + 1) * DS], dI["a_log"][dt], writes=["acol"])
    s(lambda h: h.activation(acol[:, :], acol[:, :], AF.Exp), ["acol"], ["acol"])
    v(lambda h: h.tensor_scalar_mul(acol[:, :], acol[:, :], -1.0), ["acol"], ["acol"])
    v(lambda h: h.memset(hcar[:, :], 0.0), [],
      ["hcar"] + [("hcar", dt, si) for dt in range(NDI) for si in range(DS)])
    v(lambda h: h.memset(cumcar[:, :], 0.0), [], ["cumcar"])
    v(lambda h: h.memset(cconst[:, 0:1], PI2), [], ["cconst"])
    v(lambda h: h.memset(cconst[:, 1:2], PI4), [], ["cconst"])
    v(lambda h: h.memset(cconst[:, 2:3], 1.0), [], ["cconst"])
    nc.const_aps.register(F32, PI2, cconst[:, 0:1])
    nc.const_aps.register(F32, PI4, cconst[:, 1:2])
    nc.const_aps.register(F32, 1.0, cconst[:, 2:3])

    def prep_rowscale(nat_ap, natR, natC, lhsT, e_ap, sc_scr, tag):
        ntile = natR // 128
        wn = kb.sbuf(f"wn_{tag}", [128, natC], F32)
        aw = kb.sbuf(f"aw_{tag}", [128, natC], F32)
        nw1 = natC // 32
        accA = kb.sbuf(f"accA_{tag}", [128, nw1], F32)
        accB = kb.sbuf(f"accB_{tag}", [128, 4], F32)
        sc_col = kb.sbuf(f"sccol_{tag}", [128, ntile], F32)
        K_wn, K_aw, K_A, K_B, K_sc = "p_wn", "p_aw", "p_accA", "p_accB", "p_sc"
        for tt in range(ntile):
            kb.dma("sync", wn[:, :], nat_ap[tt * 128:(tt + 1) * 128, :], writes=[K_wn])
            s(lambda h: h.activation(aw[:, :], wn[:, :], AF.Abs), [K_wn], [K_aw])
            awv = aw[:, :].rearrange("p (w k) -> p w k", k=32)
            v(lambda h, awv=awv: h.reduce_sum(accA[:, :], awv, mybir.AxisListType.X),
              [K_aw], [K_A])
            if nw1 == 32:
                def _r2(h, tt=tt):
                    h.reduce_sum(sc_col[:, tt:tt + 1],
                                 accA[:, :].rearrange("p (w k) -> p w k", k=32),
                                 mybir.AxisListType.X)
                    return h.drain()
                v(_r2, [K_A], [K_sc])
            else:
                assert nw1 == 128
                v(lambda h: h.reduce_sum(accB[:, :],
                                         accA[:, :].rearrange("p (w k) -> p w k", k=32),
                                         mybir.AxisListType.X),
                  [K_A], [K_B])
                def _r3(h, tt=tt):
                    h.reduce_sum(sc_col[:, tt:tt + 1],
                                 accB[:, :].rearrange("p (w k) -> p w k", k=4),
                                 mybir.AxisListType.X)
                    return h.drain()
                v(_r3, [K_B], [K_sc])
        inv_c = float(np.float32(1.0) / np.float32(natC))
        v(lambda h: h.tensor_scalar(sc_col[:, :], sc_col[:, :], inv_c, 1e-5,
                                    OP.mult, OP.max), [K_sc], [K_sc])
        sct = kb.sbuf(f"sct_{tag}", [ntile, 128], F32)
        t(lambda h: h.transpose(ps[0][0:ntile, 0:128], sc_col[:, :], eye32[:, :]),
          [K_sc, "eye32", ("psF", 0)], [("psD", 0)])
        v(lambda h: h.tensor_copy(sct[:, :], ps[0][0:ntile, 0:128]),
          [("psD", 0)], ["p_sct", ("psF", 0)])
        kb.dma("sync", bass.AP(sc_scr.tensor, 0, [[1, 128 * ntile]]),
               sct[:, :], reads=["p_sct"], writes=["p_scd"])
        kb.free(f"wn_{tag}", f"aw_{tag}", f"accA_{tag}", f"accB_{tag}", f"sccol_{tag}",
                f"sct_{tag}")
        kb.barrier()
        M = natR
        Kdim = natC
        ktile = Kdim // 128
        srep = kb.sbuf(f"srep_{tag}", [128, M], F32)
        hrep = kb.sbuf(f"hrep_{tag}", [128, M], F32)
        wt = [kb.sbuf(f"wt{b}_{tag}", [128, M], F32) for b in range(2)]
        g1 = kb.sbuf(f"g1_{tag}", [128, M], F32)
        webf = kb.sbuf(f"webf_{tag}", [128, M], BF16)
        kb.dma("sync", srep[:, :], bass.AP(sc_scr.tensor, 0, [[0, 128], [1, M]]),
               reads=["p_scd"], writes=["p_srep"])
        v(lambda h: h.tensor_scalar_mul(hrep[:, :], srep[:, :], 0.5),
          ["p_srep"], ["p_hrep"])
        for k in range(ktile):
            b = k % 2
            kb.dma("sync", wt[b][:, :], lhsT[k * 128:(k + 1) * 128, :],
                   reads=[("p_wt", b)], writes=[("p_wt", b)])

            def quant(h, b=b):
                h.tensor_tensor(g1[:, :], wt[b][:, :], hrep[:, :], OP.is_gt)
                h.tensor_scalar_mul(wt[b][:, :], wt[b][:, :], -1.0)
                h.tensor_tensor(wt[b][:, :], wt[b][:, :], hrep[:, :], OP.is_gt)
                h.tensor_sub(g1[:, :], g1[:, :], wt[b][:, :])
                return h.tensor_mul(webf[:, :], g1[:, :], srep[:, :])
            v(quant, [("p_wt", b), "p_hrep", "p_srep", "p_webf"],
              [("p_wt", b), "p_webf"])
            kb.dma("sync", e_ap[k * 128:(k + 1) * 128, :], webf[:, :],
                   reads=["p_webf"], writes=[("we", tag)])
        kb.free(f"srep_{tag}", f"hrep_{tag}", f"wt0_{tag}", f"wt1_{tag}",
                f"g1_{tag}", f"webf_{tag}")
        kb.barrier()

    def prep_globalscale(nat_ap, natR, natC, lhsT, e_ap, tag):
        nR, nC = natR // 32, natC // 32
        gat = kb.sbuf(f"gat_{tag}", [128, 1024], F32)
        ga = kb.sbuf(f"ga_{tag}", [128, 1024], F32)
        p1c = kb.sbuf(f"p1_{tag}", [128, 32], F32)
        sq = kb.sbuf(f"sq_{tag}", [128, 2], F32)
        assert (nR, nC) in ((128, 32), (32, 128))
        nouter = nR if nC == 128 else nC
        for R in range(nouter):
            if nC == 128:
                src = bass.AP(nat_ap.tensor, R * 32 * natC,
                              [[32, 128], [natC, 32], [1, 32]])
            else:
                src = bass.AP(nat_ap.tensor, R * 32,
                              [[32 * natC, 128], [natC, 32], [1, 32]])
            kb.dma("sync", gat[:, :], src, reads=["p_gat"], writes=["p_gat"])
            s(lambda h: h.activation(ga[:, :], gat[:, :], AF.Abs), ["p_gat"], ["p_ga"])
            def _r4(h, R=R):
                h.reduce_sum(p1c[:, R:R + 1],
                             ga[:, :].rearrange("p (w k) -> p w k", k=1024),
                             mybir.AxisListType.X)
                return h.drain()
            v(_r4, ["p_ga"], ["p_p1c"])
        if nC == 128:
            p1t = kb.sbuf(f"p1t_{tag}", [32, 128], F32)
            t(lambda h: h.transpose(ps[0][0:32, 0:128], p1c[:, :], eye32[:, :]),
              ["p_p1c", "eye32", ("psF", 0)], [("psD", 0)])
            v(lambda h: h.tensor_copy(p1t[:, :], ps[0][0:32, 0:128]),
              [("psD", 0)], ["p_p1t", ("psF", 0)])
            kb.dma("sync", bass.AP(scr["p1"].tensor, 0, [[1, 32 * 128]]),
                   p1t[:, :], reads=["p_p1t"], writes=["p_p1d"])
            kb.free(f"p1t_{tag}")
            l2src = bass.AP(scr["p1"].tensor, 0, [[32, 4], [128, 32], [1, 32]])
        else:
            kb.dma("sync", scr["p1"], p1c[:, :], reads=["p_p1c"], writes=["p_p1d"])
            l2src = bass.AP(scr["p1"].tensor, 0, [[32 * 32, 4], [32, 32], [1, 32]])
        kb.dma("sync", gat[0:4, :], l2src, reads=["p_p1d"], writes=["p_gat"])
        def _r5(h):
            h.reduce_sum(ga[0:4, 0:1],
                         gat[0:4, :].rearrange("p (w k) -> p w k", k=1024),
                         mybir.AxisListType.X)
            return h.drain()
        v(_r5, ["p_gat"], ["p_ga"])
        kb.dma("sync", bass.AP(scr["p2"].tensor, 0, [[1, 4]]), ga[0:4, 0:1],
               reads=["p_ga"], writes=["p_p2d"])
        kb.dma("sync", gat[0:1, 0:4], scr["p2"], reads=["p_p2d"], writes=["p_gat"])
        def _r6(h):
            h.reduce_sum(ga[0:1, 3:4],
                         gat[0:1, 0:4].rearrange("p (w k) -> p w k", k=4),
                         mybir.AxisListType.X)
            return h.drain()
        v(_r6, ["p_gat"], ["p_ga"])
        inv_n = float(np.float32(1.0) / np.float32(natR * natC))
        v(lambda h: h.tensor_scalar(ga[0:1, 4:5], ga[0:1, 3:4], inv_n, 1e-5,
                                    OP.mult, OP.max), ["p_ga"], ["p_ga"])
        v(lambda h: h.reciprocal(ga[0:1, 5:6], ga[0:1, 4:5]), ["p_ga"], ["p_ga"])
        def _r7(h):
            h.reciprocal(ga[0:1, 6:7], ga[0:1, 5:6])
            return h.drain()
        v(_r7, ["p_ga"], ["p_ga"])
        kb.dma("sync", bass.AP(scr["p2"].tensor, 0, [[1, 2]]), ga[0:1, 5:7],
               reads=["p_ga"], writes=["p_p2d"])
        kb.dma("sync", sq[:, :], bass.AP(scr["p2"].tensor, 0, [[0, 128], [1, 2]]),
               reads=["p_p2d"], writes=["p_sq"])
        M = lhsT.shape[1]
        ktile = lhsT.shape[0] // 128
        wt = [kb.sbuf(f"wt{b}_{tag}", [128, M], F32) for b in range(2)]
        webf = kb.sbuf(f"webf_{tag}", [128, M], BF16)
        for k in range(ktile):
            b = k % 2
            kb.dma("sync", wt[b][:, :], lhsT[k * 128:(k + 1) * 128, :],
                   reads=[("p_wt", b)], writes=[("p_wt", b)])

            def quant(h, b=b):
                h.tensor_scalar(wt[b][:, :], wt[b][:, :], sq[:, 0:1], MAGIC,
                                OP.mult, OP.add)
                h.tensor_scalar(wt[b][:, :], wt[b][:, :], MAGIC, 1.0,
                                OP.subtract, OP.min)
                return h.tensor_scalar(webf[:, :], wt[b][:, :], -1.0, sq[:, 1:2],
                                       OP.max, OP.mult)
            v(quant, [("p_wt", b), "p_sq", "p_webf"], [("p_wt", b), "p_webf"])
            kb.dma("sync", e_ap[k * 128:(k + 1) * 128, :], webf[:, :],
                   reads=["p_webf"], writes=[("we", tag)])
        kb.free(f"gat_{tag}", f"ga_{tag}", f"p1_{tag}", f"sq_{tag}",
                f"wt0_{tag}", f"wt1_{tag}", f"webf_{tag}")
        kb.barrier()

    prep_rowscale(dI["w_fc_n"], FF, D, dI["w_fc_t"], scr["w_fc_e"], scr["sc_fc"], "fc")
    prep_rowscale(dI["w_proj_n"], D, FF, dI["w_proj_t"], scr["w_proj_e"], scr["sc_proj"], "pj")
    prep_globalscale(dI["wg_n"], FF, D, dI["wg_t"], scr["wg_e"], "wg")
    prep_globalscale(dI["wu_n"], FF, D, dI["wu_t"], scr["wu_e"], "wu")
    prep_globalscale(dI["wd_n"], D, FF, dI["wd_t"], scr["wd_e"], "wd")

    RB = rowbuf

    def rows():
        m_ = RB[0:1, 0:TC]
        q_ = RB[0:1, TC:2 * TC]
        vv = RB[0:1, 2 * TC:3 * TC]
        r_ = RB[0:1, 3 * TC:4 * TC]
        c_ = RB[0:1, 4 * TC:5 * TC]
        al = RB[0:1, TC:2 * TC]
        tp = RB[0:1, 4 * TC:5 * TC]
        return m_, q_, vv, r_, c_, al, tp

    def ln_stats(src_tiles, want_alpha=False):
        n = len(src_tiles)
        for i, (ap, key) in enumerate(src_tiles):
            v(lambda h, ap=ap: h.tensor_copy(xbfs[:, :], ap), [key], ["xbfs"])
            s(lambda h, ap=ap: h.activation(x2bf[:, :], ap, AF.Square), [key], ["x2bf"])
            t(lambda h, i=i: h.matmul(ps_row[0:1, 0:TC], ones_bf[:, :], xbfs[:, :],
                                      start=(i == 0), stop=(i == n - 1)),
              ["xbfs", "ones"] + (["psrow"] if i == 0 else []), ["psr1"])
            t(lambda h, i=i: h.matmul(ps_row[0:1, TC:2 * TC], ones_bf[:, :], x2bf[:, :],
                                      start=(i == 0), stop=(i == n - 1)),
              ["x2bf", "ones"] + (["psrow"] if i == 0 else []), ["psr2"])
        m_, q_, vv, r_, c_, al, tp = rows()
        v(lambda h: h.tensor_scalar_mul(m_, ps_row[0:1, 0:TC], INV_D), ["psr1"], ["row_m"])
        v(lambda h: h.tensor_scalar_mul(q_, ps_row[0:1, TC:2 * TC], INV_D),
          ["psr2"], ["row_q", "psrow"])
        v(lambda h: h.tensor_mul(tp, m_, m_), ["row_m"], ["row_t"])
        v(lambda h: h.tensor_sub(vv, q_, tp), ["row_q", "row_t"], ["row_v"])
        v(lambda h: h.tensor_scalar_add(tp, vv, 1e-5), ["row_v"], ["row_t"])
        def _rc1(h):
            h.reciprocal(tp, tp)
            return h.drain()
        v(_rc1, ["row_t"], ["row_t"])
        def _sq1(h):
            h.activation(r_, tp, AF.Sqrt)
            return h.drain()
        s(_sq1, ["row_t"], ["row_r"])
        if want_alpha:
            v(lambda h: h.tensor_scalar_add(al, vv, 1e-5), ["row_v", "row_q"], ["row_a"])
            v(lambda h: h.reciprocal(al, al), ["row_a"], ["row_a"])
            v(lambda h: h.tensor_mul(al, al, vv), ["row_a", "row_v"], ["row_a"])
            v(lambda h: h.tensor_scalar_add(al, al, 1e-5), ["row_a"], ["row_a"])
            def _rc2(h):
                h.reciprocal(al, al)
                return h.drain()
            v(_rc2, ["row_a"], ["row_a"])
            def _sq2(h):
                h.activation(al, al, AF.Sqrt)
                return h.drain()
            s(_sq2, ["row_a"], ["row_a"])
            kb.dma("sync", scr["rows"][2:3, 0:TC], al, reads=["row_a"], writes=["d_a"], stream="row")
            kb.dma("sync", rep[:, 2 * TC:3 * TC],
                   bass.AP(scr["rows"].tensor, 2 * T, [[0, 128], [1, TC]]),
                   reads=["d_a"], writes=["rep_a"], stream="rep")
        v(lambda h: h.tensor_mul(c_, m_, r_), ["row_m", "row_r", "row_t"], ["row_c"])
        kb.dma("sync", bass.AP(scr["rows"].tensor, 0, [[T, 2], [1, TC]]),
               RB[0:1, 3 * TC:5 * TC], reads=["row_r", "row_c"],
               writes=["d_r", "d_c"], stream="row")
        kb.dma("sync", rep[:, 0:2 * TC],
               bass.AP(scr["rows"].tensor, 0, [[0, 128], [T, 2], [1, TC]]),
               reads=["d_r"], writes=["rep_r", "rep_c"], stream="rep")

    def ln_norm_xnbf(src_tiles):
        for i, (ap, key) in enumerate(src_tiles):
            dst = xnbf[:, i * TC:(i + 1) * TC]

            def f(h, ap=ap, dst=dst):
                h.tensor_mul(vtmp[:, :], ap, rep[:, 0:TC])
                return h.tensor_sub(dst, vtmp[:, :], rep[:, TC:2 * TC])
            v(f, [key, "rep_r", "rep_c", "vtmp"], [("xnbf", i), "vtmp"])

    blkctr = [0]

    def ln_out_residual(c):
        bo = [(blkout[:, i * TC:(i + 1) * TC], ("blk", i)) for i in range(ND)]
        ln_stats(bo)
        for i in range(ND):
            stc = stream[:, i * T + c * TC:i * T + (c + 1) * TC]

            def f(h, i=i, stc=stc):
                h.tensor_mul(vtmp[:, :], blkout[:, i * TC:(i + 1) * TC], rep[:, 0:TC])
                h.tensor_sub(vtmp[:, :], vtmp[:, :], rep[:, TC:2 * TC])
                return h.tensor_add(stc, stc, vtmp[:, :])
            v(f, [("blk", i), "rep_r", "rep_c", ("st", i), "vtmp"],
              [("st", i), "vtmp"])
        if DBG_ON:
            bi = blkctr[0] // NCH
            for i in range(ND):
                tap(f"b{bi}st{i}", stream[:, i * T + c * TC:i * T + (c + 1) * TC],
                    ("st", i), c, TC)
        blkctr[0] += 1

    def st_tiles_of(c):
        return [(stream[:, i * T + c * TC:i * T + (c + 1) * TC], ("st", i))
                for i in range(ND)]

    dbg = {}

    def tap(name, ap, key, c, cols, parts=128, dtype=F32):
        if not DBG_ON:
            return
        if name not in dbg:
            dbg[name] = nc.dram_tensor("dbg_" + name, [parts, NCH * cols], dtype,
                                       kind="ExternalOutput").ap()
        kb.dma("sync", dbg[name][0:parts, c * cols:(c + 1) * cols], ap,
               reads=[key], writes=[("dbgw", name, c)])

    wqk_bf = kb.sbuf("wqk_bf", [128, ND * 2 * H], BF16)
    wv_bf = kb.sbuf("wv_bf", [128, ND * D], BF16)
    wo_bf = kb.sbuf("wo_bf", [128, ND * D], BF16)
    for k in range(ND):
        kb.dma("gpsimd", wqk_bf[:, k * 32:(k + 1) * 32],
               dI["wqk"][k * 128:(k + 1) * 128, :], writes=["wqk"])
        kb.dma("gpsimd", wv_bf[:, k * D:(k + 1) * D],
               dI["wv"][k * 128:(k + 1) * 128, :], writes=["wvb"])
        kb.dma("gpsimd", wo_bf[:, k * D:(k + 1) * D],
               dI["wo"][k * 128:(k + 1) * 128, :], writes=["wob"])

    tq_s = kb.sbuf("tq_s", [16, TC], F32)
    tk_s = kb.sbuf("tk_s", [16, TC], F32)
    cosq = kb.sbuf("cosq", [16, TC], F32)
    sinq = kb.sbuf("sinq", [16, TC], F32)
    cosk = kb.sbuf("cosk", [16, TC], F32)
    sink = kb.sbuf("sink", [16, TC], F32)
    rden = kb.sbuf("rden", [16, TC], F32)
    ckv = kb.sbuf("ckv", [128, TC], F32)
    skv = kb.sbuf("skv", [128, TC], F32)
    coskv = kb.sbuf("coskv", [128, ND * TC], F32)
    sinkv = kb.sbuf("sinkv", [128, ND * TC], F32)
    heads_bf = kb.sbuf("heads_bf", [128, ND * TC], BF16)
    qrep = kb.sbuf("qrep", [128, 2 * TC], F32)

    def l0_attn_chunk(c):
        stt = st_tiles_of(c)
        ln_stats(stt)
        ln_norm_xnbf(stt)

        def qkmm(h):
            last = None
            for k in range(ND):
                h.matmul(ps_q[0:16, :], wqk_bf[:, k * 32:k * 32 + 16],
                         xnbf[:, k * TC:(k + 1) * TC],
                         start=(k == 0), stop=(k == ND - 1))
                last = h.matmul(ps[2][0:16, :], wqk_bf[:, k * 32 + 16:(k + 1) * 32],
                                xnbf[:, k * TC:(k + 1) * TC],
                                start=(k == 0), stop=(k == ND - 1))
            return last
        t(qkmm, [("xnbf", i) for i in range(ND)] + ["wqk", "psqF", ("psF", 2)],
          ["psq", ("psD", 2)])
        s(lambda h: h.activation(tq_s[:, :], ps_q[0:16, :], AF.Tanh), ["psq"],
          ["tq", "psqF"])
        s(lambda h: h.activation(tk_s[:, :], ps[2][0:16, :], AF.Tanh), [("psD", 2)],
          ["tk", ("psF", 2)])
        s(lambda h: h.activation(cosq[:, :], tq_s[:, :], AF.Sin, bias=PI2,
                                 scale=PI4), ["tq"], ["cosq"])
        s(lambda h: h.activation(sinq[:, :], tq_s[:, :], AF.Sin, bias=0.0,
                                 scale=PI4), ["tq"], ["sinq"])
        s(lambda h: h.activation(cosk[:, :], tk_s[:, :], AF.Sin, bias=PI2,
                                 scale=PI4), ["tk"], ["cosk"])
        s(lambda h: h.activation(sink[:, :], tk_s[:, :], AF.Sin, bias=0.0,
                                 scale=PI4), ["tk"], ["sink"])
        cosK, sinK = tq_s, tk_s
        def _sck(h):
            h.tensor_tensor_scan(cosK[:, :], cosk[:, :], cosk[:, :],
                                 cumcar[0:16, 16:17], OP.add, OP.bypass)
            return h.drain()
        v(_sck, ["cosk", "ccarK", "tq"], ["cosKc", "tq"])
        v(lambda h: h.tensor_copy(cumcar[0:16, 16:17], cosK[:, TC - 1:TC]),
          ["cosKc"], ["ccarK"])
        def _scs(h):
            h.tensor_tensor_scan(sinK[:, :], sink[:, :], sink[:, :],
                                 cumcar[0:16, 17:18], OP.add, OP.bypass)
            return h.drain()
        v(_scs, ["sink", "ccarS", "tk"], ["sinKc", "tk"])
        v(lambda h: h.tensor_copy(cumcar[0:16, 17:18], sinK[:, TC - 1:TC]),
          ["sinKc"], ["ccarS"])

        def denf(h):
            h.tensor_mul(rden[:, :], cosq[:, :], cosK[:, :])
            h.tensor_mul(cosK[:, :], sinq[:, :], sinK[:, :])
            h.tensor_add(rden[:, :], rden[:, :], cosK[:, :])
            h.tensor_scalar_add(rden[:, :], rden[:, :], 1e-6)
            h.drain()
            h.reciprocal(rden[:, :], rden[:, :])
            return h.drain()
        v(denf, ["cosq", "sinq", "cosKc", "sinKc"], ["den", "cosKc"])
        v(lambda h: h.tensor_mul(cosq[:, :], cosq[:, :], rden[:, :]),
          ["cosq", "den"], ["cq"])
        v(lambda h: h.tensor_mul(sinq[:, :], sinq[:, :], rden[:, :]),
          ["sinq", "den"], ["sq"])
        tap("cosk", cosk[:, :], "cosk", c, TC, 16)
        tap("sink", sink[:, :], "sink", c, TC, 16)
        tap("cosK", cosK[:, :], "cosKc", c, TC, 16)
        tap("sinK", sinK[:, :], "sinKc", c, TC, 16)
        tap("rden", rden[:, :], "den", c, TC, 16)
        tap("cq", cosq[:, :], "cq", c, TC, 16)
        kb.dma("sync", scr["qhead"][0, :, c * TC:(c + 1) * TC], cosq[:, :],
               reads=["cq"], writes=["qh0"])
        kb.dma("sync", scr["qhead"][1, :, c * TC:(c + 1) * TC], sinq[:, :],
               reads=["sq"], writes=["qh1"])
        kb.dma("sync", scr["qhead"][2, :, c * TC:(c + 1) * TC], cosk[:, :],
               reads=["cosk"], writes=["qh2"])
        kb.dma("sync", scr["qhead"][3, :, c * TC:(c + 1) * TC], sink[:, :],
               reads=["sink"], writes=["qh3"])
        for m in range(ND):
            pi = m % 2
            pb = ps[pi]

            def vmm(h, m=m, pb=pb):
                last = None
                for k in range(ND):
                    last = h.matmul(pb[:, :], wv_bf[:, k * D + m * 128:k * D + (m + 1) * 128],
                                    xnbf[:, k * TC:(k + 1) * TC],
                                    start=(k == 0), stop=(k == ND - 1))
                return last
            t(vmm, [("xnbf", i) for i in range(ND)] + ["wvb", ("psF", pi)],
              [("psD", pi)])
            kb.dma("sync", qrep[:, 0:TC],
                   bass.AP(scr["qhead"].tensor, 2 * H * T + 2 * m * T + c * TC,
                           [[T, 2], [0, 64], [1, TC]]),
                   reads=["qh2", "qrep0"], writes=["qrep0"], stream="rep")
            kb.dma("sync", qrep[:, TC:2 * TC],
                   bass.AP(scr["qhead"].tensor, 3 * H * T + 2 * m * T + c * TC,
                           [[T, 2], [0, 64], [1, TC]]),
                   reads=["qh3", "qrep1"], writes=["qrep1"], stream="rep")
            v(lambda h, pb=pb: h.tensor_mul(ckv[:, :], pb[:, :], qrep[:, 0:TC]),
              [("psD", pi), "qrep0", "ckv"], ["ckv"])
            v(lambda h, pb=pb: h.tensor_mul(skv[:, :], pb[:, :], qrep[:, TC:2 * TC]),
              [("psD", pi), "qrep1", "skv"], ["skv", ("psF", pi)])
            cs = coskv[:, m * TC:(m + 1) * TC]
            ss = sinkv[:, m * TC:(m + 1) * TC]
            def _sckv(h, cs=cs, m=m):
                h.tensor_tensor_scan(cs, ckv[:, :], ckv[:, :],
                                     cumcar[:, m:m + 1], OP.add, OP.bypass)
                return h.drain()
            v(_sckv, ["ckv", ("ccar", m)], [("coskv", m)])
            v(lambda h, cs=cs, m=m: h.tensor_copy(cumcar[:, m:m + 1], cs[:, TC - 1:TC]),
              [("coskv", m)], [("ccar", m)])
            def _sskv(h, ss=ss, m=m):
                h.tensor_tensor_scan(ss, skv[:, :], skv[:, :],
                                     cumcar[:, 8 + m:9 + m], OP.add, OP.bypass)
                return h.drain()
            v(_sskv, ["skv", ("scar", m)], [("sinkv", m)])
            v(lambda h, ss=ss, m=m: h.tensor_copy(cumcar[:, 8 + m:9 + m], ss[:, TC - 1:TC]),
              [("sinkv", m)], [("scar", m)])
        for m in range(ND):
            kb.dma("sync", qrep[:, 0:TC],
                   bass.AP(scr["qhead"].tensor, 2 * m * T + c * TC,
                           [[T, 2], [0, 64], [1, TC]]),
                   reads=["qh0", "qrep0"], writes=["qrep0"], stream="rep")
            kb.dma("sync", qrep[:, TC:2 * TC],
                   bass.AP(scr["qhead"].tensor, H * T + 2 * m * T + c * TC,
                           [[T, 2], [0, 64], [1, TC]]),
                   reads=["qh1", "qrep1"], writes=["qrep1"], stream="rep")
            hd = heads_bf[:, m * TC:(m + 1) * TC]

            def f(h, m=m, hd=hd):
                h.tensor_mul(ckv[:, :], coskv[:, m * TC:(m + 1) * TC], qrep[:, 0:TC])
                h.tensor_mul(skv[:, :], sinkv[:, m * TC:(m + 1) * TC], qrep[:, TC:2 * TC])
                return h.tensor_add(hd, ckv[:, :], skv[:, :])
            v(f, [("coskv", m), ("sinkv", m), "qrep0", "qrep1", "ckv", "skv"],
              [("heads", m), "ckv", "skv"])
        tap("coskv0", coskv[:, 0:TC], ("coskv", 0), c, TC)
        tap("sinkv0", sinkv[:, 0:TC], ("sinkv", 0), c, TC)
        tap("heads0", heads_bf[:, 0:TC], ("heads", 0), c, TC, 128, BF16)
        tap("xnbf0", xnbf[:, 0:TC], ("xnbf", 0), c, TC, 128, BF16)
        for mo in range(ND):
            pi = 2 + mo % 2
            pb = ps[pi]

            def omm(h, mo=mo, pb=pb):
                last = None
                for k in range(ND):
                    last = h.matmul(pb[:, :], wo_bf[:, k * D + mo * 128:k * D + (mo + 1) * 128],
                                    heads_bf[:, k * TC:(k + 1) * TC],
                                    start=(k == 0), stop=(k == ND - 1))
                return last
            t(omm, [("heads", k) for k in range(ND)] + ["wob", ("psF", pi)],
              [("psD", pi)])
            s(lambda h, mo=mo, pb=pb: h.activation(blkout[:, mo * TC:(mo + 1) * TC],
                                                   pb[:, :], AF.Copy),
              [("psD", pi)], [("blk", mo), ("psF", pi)])
        ln_out_residual(c)

    if "a" in blocks:
        for c in range(NCH):
            l0_attn_chunk(c)

    kb.free("wqk_bf", "wv_bf", "wo_bf", "tq_s", "tk_s", "cosq", "sinq", "cosk",
            "sink", "rden", "ckv", "skv", "coskv", "sinkv", "heads_bf", "qrep")
    kb.barrier()

    vt_bf = kb.sbuf("vt_bf", [128, ND * 16], BF16)
    fb_bf = kb.sbuf("fb_bf", [16, 16], BF16)
    ut_bf = kb.sbuf("ut_bf", [16, FF], BF16)
    for k in range(ND):
        kb.dma("gpsimd", vt_bf[:, k * 16:(k + 1) * 16],
               dI["v_t"][k * 128:(k + 1) * 128, :], writes=["vtb"])
    kb.dma("gpsimd", fb_bf[:, :], dI["fb_t"], writes=["fbb"])
    kb.dma("gpsimd", ut_bf[:, :], dI["u_t"], writes=["utb"])
    xfb_bf = kb.sbuf("xfb_bf", [16, TC], BF16)
    rr_bf = kb.sbuf("rr_bf", [128, NFF * TC], BF16)
    wstr = [kb.sbuf(f"wstr{b}", [128, 1024], BF16) for b in range(2)]
    stmp = kb.sbuf("stmp", [128, TC], F32)

    def l0_ffn_chunk(c):
        stt = st_tiles_of(c)
        ln_stats(stt)
        ln_norm_xnbf(stt)

        def xvmm(h):
            last = None
            for k in range(ND):
                last = h.matmul(ps_q[0:16, :], vt_bf[:, k * 16:(k + 1) * 16],
                                xnbf[:, k * TC:(k + 1) * TC],
                                start=(k == 0), stop=(k == ND - 1))
            return last
        t(xvmm, [("xnbf", i) for i in range(ND)] + ["vtb", "psqF"], ["psq"])
        s(lambda h: h.activation(xfb_bf[:, :], ps_q[0:16, :], AF.Copy),
          ["psq"], ["xvbf", "psqF"])
        t(lambda h: h.matmul(ps[2][0:16, :], fb_bf[:, :], xfb_bf[:, :],
                             start=True, stop=True),
          ["xvbf", "fbb", ("psF", 2)], [("psD", 2)])
        v(lambda h: h.tensor_scalar_mul(xfb_bf[:, :], ps[2][0:16, :], gcol[0:16, 0:1]),
          [("psD", 2), "gatec", "xvbf"], ["xfbg", ("psF", 2)])
        for mo in range(NFF):
            b = mo % 2
            kb.dma("sync", wstr[b][:, :], lhsT_ap(scr["w_fc_e"], FF, 0, mo * 128, 8),
                   reads=[("wstrF", b)], writes=[("wstrD", b)], stream=f"wl{b}")
            pb = ps[b]

            def hmm(h, mo=mo, b=b, pb=pb):
                for k in range(ND):
                    h.matmul(pb[:, :], wstr[b][:, k * 128:(k + 1) * 128],
                             xnbf[:, k * TC:(k + 1) * TC],
                             start=(k == 0), stop=False)
                return h.matmul(pb[:, :], ut_bf[:, mo * 128:(mo + 1) * 128],
                                xfb_bf[:, :], start=False, stop=True)
            t(hmm, [("wstrD", b), "xfbg", "utb", ("psF", b)]
              + [("xnbf", i) for i in range(ND)], [("psD", b), ("wstrF", b)])
            s(lambda h, pb=pb: h.activation(stmp[:, :], pb[:, :], AF.Relu),
              [("psD", b), "stmp"], ["stmp"])
            s(lambda h, mo=mo: h.activation(rr_bf[:, mo * TC:(mo + 1) * TC],
                                            stmp[:, :], AF.Square),
              ["stmp"], [("rr", mo), ("psF", b)])
        tap("rr0", rr_bf[:, 0:TC], ("rr", 0), c, TC, 128, BF16)
        tap("rr31", rr_bf[:, 31 * TC:32 * TC], ("rr", 31), c, TC, 128, BF16)
        tap("xfbg", xfb_bf[:, :], "xfbg", c, TC, 16, BF16)
        for mo in range(ND):
            pi = 2 + mo % 2
            pb = ps[pi]
            for kg in range(4):
                b = kg % 2
                kb.dma("sync", wstr[b][:, :],
                       lhsT_ap(scr["w_proj_e"], D, kg * 1024, mo * 128, 8),
                       reads=[("wstrF", b)], writes=[("wstrD", b)], stream=f"wl{b}")

                def fmm(h, kg=kg, b=b, pb=pb):
                    last = None
                    for kk in range(8):
                        k = kg * 8 + kk
                        last = h.matmul(pb[:, :], wstr[b][:, kk * 128:(kk + 1) * 128],
                                        rr_bf[:, k * TC:(k + 1) * TC],
                                        start=(k == 0), stop=(k == NFF - 1))
                    return last
                t(fmm, [("wstrD", b)] + [("rr", kg * 8 + kk) for kk in range(8)]
                  + ([("psF", pi)] if kg == 0 else [("psD", pi)]),
                  [("psD", pi), ("wstrF", b)])
            s(lambda h, mo=mo, pb=pb: h.activation(blkout[:, mo * TC:(mo + 1) * TC],
                                                   pb[:, :], AF.Copy),
              [("psD", pi)], [("blk", mo), ("psF", pi)])
        ln_out_residual(c)

    if "f" in blocks:
        for c in range(NCH):
            l0_ffn_chunk(c)

    kb.free("vt_bf", "fb_bf", "ut_bf", "xfb_bf", "rr_bf", "wstr0", "wstr1", "stmp")
    kb.barrier()

    xi_sb = kb.sbuf("xi_sb", [128, NDI * (TC + 3)], BF16)
    tails = kb.sbuf("tails", [128, NDI * 3], BF16)
    zs_bf = kb.sbuf("zs_bf", [128, NDI * TC], BF16)
    brep = kb.sbuf("brep", [128, DS * TC], BF16)
    crep = kb.sbuf("crep", [128, DS * TC], BF16)
    delta_t = kb.sbuf("delta_t", [128, TC], F32)
    u_t = kb.sbuf("u_sb", [128, TC], BF16)
    NB = 3
    da_t = [kb.sbuf(f"da_t{i}", [128, TC], BF16) for i in range(NB)]
    ub_t = [kb.sbuf(f"ub_t{i}", [128, TC], BF16) for i in range(NB)]
    h_t = [kb.sbuf(f"h_t{i}", [128, TC], F32) for i in range(2)]
    ch_t = [kb.sbuf(f"ch_t{i}", [128, TC], BF16) for i in range(NB)]
    dbc_f = kb.sbuf("dbc_f", [96, TC], F32)
    bc_bf = kb.sbuf("bc_bf", [32, TC], BF16)
    dt_bfq = kb.sbuf("dt_bfq", [64, TC], BF16)
    convw_c = kb.sbuf("convw_c", [128, NDI * 4], F32)
    dcol_c = kb.sbuf("dcol_c", [128, NDI], F32)
    xp_bf = kb.sbuf("xp_bf", [128, NDI * 96], BF16)
    dtw_bf = kb.sbuf("dtw_bf", [64, DI], BF16)
    wstr2 = [kb.sbuf(f"wstr2_{b}", [128, 1024], BF16) for b in range(2)]
    mtmp = vtmp

    kb.dma("sync", convw_c[:, :], dI["convw"], writes=["cvd"])
    kb.dma("sync", dcol_c[:, :], dI["dcol"], writes=["ddg"])
    for dt in range(NDI):
        kb.dma("gpsimd", xp_bf[:, dt * 96:(dt + 1) * 96],
               dI["x_proj"][dt * 128:(dt + 1) * 128, :], writes=["xpb"])
    kb.dma("gpsimd", dtw_bf[:, :], dI["dt_w"], writes=["dtwb"])
    def l1_attn_chunk(c):
        stt = st_tiles_of(c)
        ln_stats(stt, want_alpha=True)
        for i, (ap, key) in enumerate(stt):
            xl = blkout[:, i * TC:(i + 1) * TC]

            def f(h, ap=ap, xl=xl, i=i):
                h.tensor_mul(xl, ap, rep[:, 0:TC])
                h.tensor_sub(xl, xl, rep[:, TC:2 * TC])
                return h.tensor_mul(xnbf[:, i * TC:(i + 1) * TC], xl, rep[:, 2 * TC:3 * TC])
            v(f, [key, "rep_r", "rep_c", "rep_a"], [("blk", i), ("xnbf", i)])
        for mo in range(2 * NDI):
            b = mo % 2
            kb.dma("gpsimd", wstr2[b][:, :], lhsT_ap(dI["in_proj"], 2 * DI, 0, mo * 128, 8),
                   reads=[("w2F", b)], writes=[("w2D", b)], stream=f"wl2{b}")

            def fold(h, b=b):
                last = None
                for k in range(ND):
                    last = h.tensor_scalar_mul(wstr2[b][:, k * 128:(k + 1) * 128],
                                               wstr2[b][:, k * 128:(k + 1) * 128],
                                               rms_c[:, k:k + 1])
                return last
            v(fold, [("w2D", b), "rms"], [("w2D", b)])
            pb = ps[b]

            def xzmm(h, b=b, pb=pb):
                last = None
                for k in range(ND):
                    last = h.matmul(pb[:, :], wstr2[b][:, k * 128:(k + 1) * 128],
                                    xnbf[:, k * TC:(k + 1) * TC],
                                    start=(k == 0), stop=(k == ND - 1))
                return last
            t(xzmm, [("w2D", b), ("psF", b)] + [("xnbf", i) for i in range(ND)],
              [("psD", b), ("w2F", b)])
            if mo < NDI:
                base = mo * (TC + 3)

                def evxi(h, pb=pb, base=base, mo=mo):
                    h.tensor_copy(xi_sb[:, base:base + 3], tails[:, mo * 3:(mo + 1) * 3])
                    return h.tensor_copy(xi_sb[:, base + 3:base + 3 + TC], pb[:, :])
                v(evxi, [("psD", b), ("xi", mo), ("tl", mo)], [("xi", mo), ("psF", b)])
            else:
                dtt = mo - NDI
                s(lambda h, pb=pb, dtt=dtt: h.activation(zs_bf[:, dtt * TC:(dtt + 1) * TC],
                                                         pb[:, :], AF.Silu),
                  [("psD", b)], [("zs", dtt), ("psF", b)])
        for dt in range(NDI):
            base = dt * (TC + 3)

            def sv_tail(h, dt=dt, base=base):
                return h.tensor_copy(tails[:, dt * 3:(dt + 1) * 3],
                                     xi_sb[:, base + TC:base + TC + 3])
            v(sv_tail, [("xi", dt)], [("tl", dt)])

            def conv(h, dt=dt, base=base):
                h.tensor_scalar_mul(mtmp[:, :], xi_sb[:, base:base + TC],
                                    convw_c[:, dt * 4:dt * 4 + 1])
                for j in range(1, 4):
                    h.scalar_tensor_tensor(mtmp[:, :], xi_sb[:, base + j:base + j + TC],
                                           convw_c[:, dt * 4 + j:dt * 4 + j + 1],
                                           mtmp[:, :], OP.mult, OP.add)
                return h.tensor_scalar_add(mtmp[:, :], mtmp[:, :], convb_c[:, dt:dt + 1])
            v(conv, [("xi", dt), "cvd", "convb", "vtmp", ("tl", dt)], ["vtmp"])
            s(lambda h, dt=dt, base=base: h.activation(xi_sb[:, base + 3:base + 3 + TC],
                                                       mtmp[:, :], AF.Silu),
              ["vtmp", ("xi", dt)], [("xi", dt)])
        def dbcmm(h):
            last = None
            for k in range(NDI):
                last = h.matmul(ps_q[0:96, :], xp_bf[:, k * 96:(k + 1) * 96],
                                xi_sb[:, k * (TC + 3) + 3:k * (TC + 3) + 3 + TC],
                                start=(k == 0), stop=(k == NDI - 1))
            return last
        t(dbcmm, [("xi", k) for k in range(NDI)] + ["xpb", "psqF"], ["psq"])

        def evdbc(h):
            h.tensor_copy(dbc_f[:, :], ps_q[0:96, :])
            return h.tensor_copy(dt_bfq[:, :], dbc_f[0:64, :])
        v(evdbc, ["psq"], ["dbc", "dtq", "psqF"])
        v(lambda h: h.tensor_copy(bc_bf[:, :], dbc_f[64:96, :]), ["dbc", "bcbf"], ["bcbf"])
        kb.dma("sync", scr["bc"][0, :, c * TC:(c + 1) * TC], bc_bf[0:16, :],
               reads=["bcbf"], writes=["bcd0"], stream="row")
        kb.dma("sync", scr["bc"][1, :, c * TC:(c + 1) * TC], bc_bf[16:32, :],
               reads=["bcbf"], writes=["bcd1"], stream="row")
        kb.dma("sync", brep[:, :],
               bass.AP(scr["bc"].tensor, c * TC, [[0, 128], [T, DS], [1, TC]]),
               reads=["bcd0"] + [("brep", si) for si in range(DS)],
               writes=[("brep", si) for si in range(DS)], stream="rep")
        kb.dma("sync", crep[:, :],
               bass.AP(scr["bc"].tensor, DS * T + c * TC, [[0, 128], [T, DS], [1, TC]]),
               reads=["bcd1"] + [("crep", si) for si in range(DS)],
               writes=[("crep", si) for si in range(DS)], stream="rep")
        for dt in range(NDI):
            xcs_ap = xi_sb[:, dt * (TC + 3) + 3:dt * (TC + 3) + 3 + TC]
            t(lambda h, dt=dt: h.matmul(ps[2][:, :], dtw_bf[:, dt * 128:(dt + 1) * 128],
                                        dt_bfq[:, :], start=True, stop=True),
              ["dtq", "dtwb", ("psF", 2)], [("psD", 2)])
            s(lambda h, dt=dt: h.activation(mtmp[:, :], ps[2][:, :], AF.Exp,
                                            bias=dtb_c[:, dt:dt + 1]),
              [("psD", 2), "dtb", "vtmp"], ["vtmp", ("psF", 2)])
            s(lambda h: h.activation(delta_t[:, :], mtmp[:, :], AF.Ln, bias=1.0),
              ["vtmp", "delta"], ["delta"])
            v(lambda h, xcs_ap=xcs_ap: h.tensor_mul(u_t[:, :], delta_t[:, :], xcs_ap),
              ["delta", ("xi", dt), "u_t"], ["u_t"])
            for si in range(DS):
                di2 = si % NB
                dh = si % 2
                s(lambda h, dt=dt, si=si, di2=di2: h.activation(
                    da_t[di2][:, :], delta_t[:, :], AF.Exp,
                    scale=acol[:, dt * DS + si:dt * DS + si + 1]),
                  ["delta", "acol", ("da", di2)], [("da", di2)])
                v(lambda h, si=si, di2=di2: h.tensor_mul(ub_t[di2][:, :], u_t[:, :],
                                                         brep[:, si * TC:(si + 1) * TC]),
                  ["u_t", ("brep", si), ("ub", di2)], [("ub", di2)])
                def _sch(h, dt=dt, si=si, di2=di2, dh=dh):
                    h.tensor_tensor_scan(
                        h_t[dh][:, :], da_t[di2][:, :], ub_t[di2][:, :],
                        hcar[:, dt * DS + si:dt * DS + si + 1], OP.mult, OP.add)
                    return h.drain()
                v(_sch, [("da", di2), ("ub", di2), ("hcar", dt, si), ("h", dh)],
                  [("h", dh)], drained=True)
                def _chc(h, dt=dt, si=si, di2=di2, dh=dh):
                    h.tensor_copy(hcar[:, dt * DS + si:dt * DS + si + 1],
                                  h_t[dh][:, TC - 1:TC])
                    h.tensor_mul(ch_t[di2][:, :], h_t[dh][:, :],
                                 crep[:, si * TC:(si + 1) * TC])
                    return h.drain()
                g(_chc, [("h", dh), ("crep", si), ("ch", di2), ("hcar", dt, si)],
                  [("ch", di2), ("hcar", dt, si)], drained=True)
                t(lambda h, si=si, di2=di2: h.matmul(ps[3][:, :], eye_bf[:, :],
                                                     ch_t[di2][:, :],
                                                     start=(si == 0), stop=(si == DS - 1)),
                  [("ch", di2), "eye"] + ([("psF", 3)] if si == 0 else [("psD", 3)]),
                  [("psD", 3)])
            def yzf(h, dt=dt, xcs_ap=xcs_ap):
                h.scalar_tensor_tensor(mtmp[:, :], xcs_ap, dcol_c[:, dt:dt + 1],
                                       ps[3][:, :], OP.mult, OP.add)
                return h.tensor_mul(zs_bf[:, dt * TC:(dt + 1) * TC], mtmp[:, :],
                                    zs_bf[:, dt * TC:(dt + 1) * TC])
            v(yzf, [("psD", 3), ("xi", dt), "ddg", ("zs", dt), "vtmp"],
              [("zs", dt), ("psF", 3), "vtmp"])
        for mo in range(ND):
            b = mo % 2
            pb = ps[b]
            for kg in range(2):
                kb.dma("gpsimd", wstr2[b][:, :], lhsT_ap(dI["w_out"], D, kg * 1024, mo * 128, 8),
                       reads=[("w2F", b)], writes=[("w2D", b)], stream=f"wl2{b}")

                def ymm(h, kg=kg, b=b, pb=pb):
                    last = None
                    for kk in range(8):
                        k = kg * 8 + kk
                        last = h.matmul(pb[:, :], wstr2[b][:, kk * 128:(kk + 1) * 128],
                                        zs_bf[:, k * TC:(k + 1) * TC],
                                        start=(k == 0), stop=(k == NDI - 1))
                    return last
                t(ymm, [("w2D", b)] + [("zs", kg * 8 + kk) for kk in range(8)]
                  + ([("psF", b)] if kg == 0 else [("psD", b)]),
                  [("psD", b), ("w2F", b)])
            v(lambda h, mo=mo, pb=pb: h.tensor_add(blkout[:, mo * TC:(mo + 1) * TC],
                                                   blkout[:, mo * TC:(mo + 1) * TC],
                                                   pb[:, :]),
              [("psD", b), ("blk", mo)], [("blk", mo), ("psF", b)])
        ln_out_residual(c)

    for dt in range(NDI):
        v(lambda h, dt=dt: h.memset(tails[:, dt * 3:(dt + 1) * 3], 0.0), [], [("tl", dt)])

    if "m" in blocks:
        for c in range(NCH):
            l1_attn_chunk(c)

    kb.free("xi_sb", "tails", "zs_bf", "brep", "crep", "delta_t", "bc_bf",
            "u_sb", *[f"{nm}{i}" for nm in ("da_t", "ub_t", "ch_t")
                      for i in range(NB)], "h_t0", "h_t1",
            "dbc_f", "dt_bfq", "convw_c", "dcol_c", "xp_bf", "dtw_bf",
            "wstr2_0", "wstr2_1")
    kb.barrier()

    xq_bf = kb.sbuf("xq_bf", [128, ND * TC], BF16)
    sg_bf = kb.sbuf("sg_bf", [128, NFF * TC], BF16)
    r_bf = kb.sbuf("r_bf", [128, NFF * TC], BF16)
    amax = kb.sbuf("amax", [128, TC], F32)
    amax2 = kb.sbuf("amax2", [128, TC], F32)
    srepq = kb.sbuf("srepq", [128, 2 * TC], F32)
    arow_sb = kb.sbuf("arow_sb", [TC // 128, 128], F32)
    wstr3 = [kb.sbuf(f"wstr3_{b}", [128, 1024], BF16) for b in range(2)]
    qtmp = kb.sbuf("qtmp", [128, TC], F32)

    def act_quant(src_keys, src_aps, dst_aps, dst_keys):
        n = len(src_aps)

        def trees(h):
            h.tensor_tensor(amax[:, :], src_aps[0], src_aps[1], OP.max)
            for i in range(2, n):
                h.tensor_tensor(amax[:, :], amax[:, :], src_aps[i], OP.max)
            h.tensor_tensor(amax2[:, :], src_aps[0], src_aps[1], OP.min)
            for i in range(2, n):
                h.tensor_tensor(amax2[:, :], amax2[:, :], src_aps[i], OP.min)
            h.tensor_scalar_mul(amax2[:, :], amax2[:, :], -1.0)
            return h.tensor_tensor(amax[:, :], amax[:, :], amax2[:, :], OP.max)
        v(trees, src_keys + ["amax", "amax2"], ["amax", "amax2"])
        def trs(h):
            last = None
            for j in range(TC // 128):
                last = h.transpose(ps[2][:, j * 128:(j + 1) * 128],
                                   amax[:, j * 128:(j + 1) * 128], eye32[:, :])
            return last
        t(trs, ["amax", "eye32", ("psF", 2)], [("psD", 2)])
        pv = ps[2][:, :].rearrange("p (j k) -> p j k", k=128)
        def _r8(h):
            h.reduce_max(amax2[:, 0:TC // 128], pv, mybir.AxisListType.X)
            return h.drain()
        v(_r8, [("psD", 2), "amax2"], ["amax2", ("psF", 2)])
        t(lambda h: h.transpose(ps[3][0:TC // 128, 0:128], amax2[:, 0:TC // 128],
                                eye32[:, :]),
          ["amax2", "eye32", ("psF", 3)], [("psD", 3)])
        v(lambda h: h.tensor_copy(arow_sb[:, :], ps[3][0:TC // 128, 0:128]),
          [("psD", 3), "arow"], ["arow", ("psF", 3)])
        kb.dma("sync", bass.AP(scr["rows"].tensor, 3 * T, [[1, TC]]),
               arow_sb[:, :], reads=["arow"], writes=["d_aq"], stream="row")
        kb.dma("sync", amax[:, :], bass.AP(scr["rows"].tensor, 3 * T, [[0, 128], [1, TC]]),
               reads=["d_aq", "amax"], writes=["amax"], stream="rep")

        def mkscale(h):
            h.tensor_scalar_max(amax[:, :], amax[:, :], 1e-5)
            h.drain()
            h.reciprocal(srepq[:, 0:TC], amax[:, :])
            h.drain()
            h.tensor_scalar_mul(srepq[:, 0:TC], srepq[:, 0:TC], 127.0)
            h.tensor_scalar_mul(srepq[:, TC:2 * TC], amax[:, :], INV127)
            return h.drain()
        v(mkscale, ["amax", "srepq"], ["srepq", "amax"])
        for i in range(n):
            def f(h, i=i):
                h.tensor_mul(qtmp[:, :], src_aps[i], srepq[:, 0:TC])
                h.tensor_scalar(qtmp[:, :], qtmp[:, :], MAGIC, MAGIC, OP.add, OP.subtract)
                return h.tensor_mul(dst_aps[i], qtmp[:, :], srepq[:, TC:2 * TC])
            v(f, [src_keys[i], "srepq", "qtmp"], [dst_keys[i], "qtmp"])

    def l1_ffn_chunk(c):
        stt = st_tiles_of(c)
        ln_stats(stt)
        for i, (ap, key) in enumerate(stt):
            xn_ = blkout[:, i * TC:(i + 1) * TC]

            def f(h, ap=ap, xn_=xn_):
                h.tensor_mul(xn_, ap, rep[:, 0:TC])
                return h.tensor_sub(xn_, xn_, rep[:, TC:2 * TC])
            v(f, [key, "rep_r", "rep_c", ("blk", i)], [("blk", i)])
        act_quant([("blk", i) for i in range(ND)],
                  [blkout[:, i * TC:(i + 1) * TC] for i in range(ND)],
                  [xq_bf[:, i * TC:(i + 1) * TC] for i in range(ND)],
                  [("xq", i) for i in range(ND)])
        tap("aq_amax", amax[:, :], "amax", c, TC)
        tap("aq_s", srepq[:, 0:TC], "srepq", c, TC)
        tap("xn0", blkout[:, 0:TC], ("blk", 0), c, TC)
        tap("xq0", xq_bf[:, 0:TC], ("xq", 0), c, TC, 128, BF16)
        for mo in range(NFF):
            b = mo % 2
            kb.dma("sync", wstr3[b][:, :], lhsT_ap(scr["wg_e"], FF, 0, mo * 128, 8),
                   reads=[("w3F", b)], writes=[("w3D", b)], stream=f"wl{b}")
            pb = ps[b]

            def gmm(h, b=b, pb=pb):
                last = None
                for k in range(ND):
                    last = h.matmul(pb[:, :], wstr3[b][:, k * 128:(k + 1) * 128],
                                    xq_bf[:, k * TC:(k + 1) * TC],
                                    start=(k == 0), stop=(k == ND - 1))
                return last
            t(gmm, [("w3D", b), ("psF", b)] + [("xq", i) for i in range(ND)],
              [("psD", b), ("w3F", b)])
            s(lambda h, mo=mo, pb=pb: h.activation(sg_bf[:, mo * TC:(mo + 1) * TC],
                                                   pb[:, :], AF.Silu),
              [("psD", b)], [("sg", mo), ("psF", b)])
        for mo in range(NFF):
            b = mo % 2
            kb.dma("sync", wstr3[b][:, :], lhsT_ap(scr["wu_e"], FF, 0, mo * 128, 8),
                   reads=[("w3F", b)], writes=[("w3D", b)], stream=f"wl{b}")
            pb = ps[b]

            def umm(h, b=b, pb=pb):
                last = None
                for k in range(ND):
                    last = h.matmul(pb[:, :], wstr3[b][:, k * 128:(k + 1) * 128],
                                    xq_bf[:, k * TC:(k + 1) * TC],
                                    start=(k == 0), stop=(k == ND - 1))
                return last
            t(umm, [("w3D", b), ("psF", b)] + [("xq", i) for i in range(ND)],
              [("psD", b), ("w3F", b)])
            v(lambda h, mo=mo, pb=pb: h.tensor_mul(r_bf[:, mo * TC:(mo + 1) * TC],
                                                   pb[:, :], sg_bf[:, mo * TC:(mo + 1) * TC]),
              [("psD", b), ("sg", mo)], [("r", mo), ("psF", b)])
        tap("sg0", sg_bf[:, 0:TC], ("sg", 0), c, TC, 128, BF16)
        tap("rpre0", r_bf[:, 0:TC], ("r", 0), c, TC, 128, BF16)
        act_quant([("r", i) for i in range(NFF)],
                  [r_bf[:, i * TC:(i + 1) * TC] for i in range(NFF)],
                  [r_bf[:, i * TC:(i + 1) * TC] for i in range(NFF)],
                  [("r", i) for i in range(NFF)])
        tap("rq0", r_bf[:, 0:TC], ("r", 0), c, TC, 128, BF16)
        for mo in range(ND):
            pi = 2 + mo % 2
            pb = ps[pi]
            for kg in range(4):
                b = kg % 2
                kb.dma("sync", wstr3[b][:, :],
                       lhsT_ap(scr["wd_e"], D, kg * 1024, mo * 128, 8),
                       reads=[("w3F", b)], writes=[("w3D", b)], stream=f"wl{b}")

                def dmm(h, kg=kg, b=b, pb=pb):
                    last = None
                    for kk in range(8):
                        k = kg * 8 + kk
                        last = h.matmul(pb[:, :], wstr3[b][:, kk * 128:(kk + 1) * 128],
                                        r_bf[:, k * TC:(k + 1) * TC],
                                        start=(k == 0), stop=(k == NFF - 1))
                    return last
                t(dmm, [("w3D", b)] + [("r", kg * 8 + kk) for kk in range(8)]
                  + ([("psF", pi)] if kg == 0 else [("psD", pi)]),
                  [("psD", pi), ("w3F", b)])
            s(lambda h, mo=mo, pb=pb: h.activation(blkout[:, mo * TC:(mo + 1) * TC],
                                                   pb[:, :], AF.Copy),
              [("psD", pi)], [("blk", mo), ("psF", pi)])
        ln_out_residual(c)

    if "g" in blocks:
        for c in range(NCH):
            l1_ffn_chunk(c)

    for i in range(ND):
        kb.dma("sync", out[i * 128:(i + 1) * 128, :], stream[:, i * T:(i + 1) * T],
               reads=[("st", i)], writes=[("out", i)])
    kb.wait_only("sync", [("out", i) for i in range(ND)])

    with nc.Block() as block:
        kb.flush(block)
    kb.close()
    return nc



def _prep_inputs(inputs):
    F = np.float32

    def c(x):
        return np.ascontiguousarray(np.asarray(x, dtype=F))

    WQ = c(inputs["L0_attn_W_Q"])[:, :, 0].T
    WK = c(inputs["L0_attn_W_K"])[:, :, 0].T
    wqk = np.ascontiguousarray(np.concatenate([WQ, WK], axis=1))
    wv = np.ascontiguousarray(c(inputs["L0_attn_W_V"]).transpose(1, 0, 2).reshape(D, H * DH))
    wo = np.ascontiguousarray(c(inputs["L0_attn_W_O"]).transpose(2, 1, 0).reshape(H * DH, D))
    conv_w = c(inputs["L1_attn_conv_w"])
    convw = np.ascontiguousarray(conv_w.reshape(NDI, 128, 4).transpose(1, 0, 2).reshape(128, NDI * 4))
    Dp = c(inputs["L1_attn_D"])
    dcol = np.ascontiguousarray(Dp.reshape(NDI, 128).T)
    com = {
        "wqk": wqk, "wv": wv, "wo": wo,
        "w_fc_n": c(inputs["L0_ffn_w_fc"]),
        "w_fc_t": np.ascontiguousarray(c(inputs["L0_ffn_w_fc"]).T),
        "w_proj_n": c(inputs["L0_ffn_w_proj"]),
        "w_proj_t": np.ascontiguousarray(c(inputs["L0_ffn_w_proj"]).T),
        "u_t": np.ascontiguousarray(c(inputs["L0_ffn_u"]).T),
        "v_t": np.ascontiguousarray(c(inputs["L0_ffn_v"]).T),
        "fb_t": np.ascontiguousarray(c(inputs["L0_ffn_fast_b"]).T),
        "gate": np.asarray(inputs["L0_ffn_gate"], F).reshape(1, 1),
        "rms_w": np.ascontiguousarray(c(inputs["L1_attn_rms_w"]).reshape(ND, 128).T),
        "in_proj": c(inputs["L1_attn_in_proj"]),
        "convw": convw,
        "conv_b": np.ascontiguousarray(c(inputs["L1_attn_conv_b"]).reshape(NDI, 128).T),
        "x_proj": c(inputs["L1_attn_x_proj"]),
        "dt_w": c(inputs["L1_attn_dt_w"]),
        "dt_b": np.ascontiguousarray(c(inputs["L1_attn_dt_b"]).reshape(NDI, 128).T),
        "a_log": np.ascontiguousarray(c(inputs["L1_attn_A_log"]).reshape(NDI, 128, DS)),
        "dcol": dcol,
        "w_out": c(inputs["L1_attn_out_proj"]),
        "wg_n": c(inputs["L1_ffn_w_gate"]),
        "wg_t": np.ascontiguousarray(c(inputs["L1_ffn_w_gate"]).T),
        "wu_n": c(inputs["L1_ffn_w_up"]),
        "wu_t": np.ascontiguousarray(c(inputs["L1_ffn_w_up"]).T),
        "wd_n": c(inputs["L1_ffn_w_down"]),
        "wd_t": np.ascontiguousarray(c(inputs["L1_ffn_w_down"]).T),
        "eye": np.eye(128, dtype=ml_dtypes.bfloat16),
        "ones": np.ones((128, 1), dtype=ml_dtypes.bfloat16),
    }
    x = np.asarray(inputs["L0_attn_query"], F)
    in_maps = []
    for b in range(8):
        m = dict(com)
        m["x"] = np.ascontiguousarray(x[b].T)
        in_maps.append(m)
    return in_maps


def kernel(_blocks=("a", "f", "m", "g"), **inputs):
    in_maps = _prep_inputs(inputs)
    nc = build_program(_blocks)
    res = run_bass_kernel_spmd(nc, in_maps, core_ids=list(range(8)))
    outs = [np.ascontiguousarray(np.asarray(r["y"]).T) for r in res.results]
    return np.stack(outs).astype(np.float32)


